# version 07
# baseline (speedup 1.0000x reference)
import os
import numpy as np
from contextlib import ExitStack

import concourse.bass as bass
import concourse.mybir as mybir
from concourse.bass_utils import run_bass_kernel_spmd

F32 = mybir.dt.float32
BF16 = mybir.dt.bfloat16
AF = mybir.ActivationFunctionType
ALU = mybir.AluOpType
AX = mybir.AxisListType

P = 128
TOK = 2048
NCH = 8
NTT = 16
NT4 = 4
EPS_Z = 1e-8
EPS_LN = 1e-5
SCALING = 2.0


class Sem:
    def __init__(self, h):
        self.h = h
        self.n = 0

    def bump(self, k=1):
        self.n += k
        return self.n


class Prog:
    ENGS = ("tensor", "vector", "scalar", "gpsimd", "sync")

    def __init__(self):
        self.q = {e: [] for e in self.ENGS}
        self.lastwait = {}
        self.trace = {e: [] for e in self.ENGS}

    def wait(self, eng, mark):
        if mark is None:
            return
        sem, val = mark
        key = (eng, id(sem))
        if self.lastwait.get(key, -1) >= val:
            return
        self.lastwait[key] = val
        self.trace[eng].append(("wait", id(sem), val))
        self.q[eng].append(lambda e, s=sem, v=val: e.wait_ge(s.h, v))

    def op(self, eng, fn, sig=None, k=1, waits=(), fence=0):
        for m in waits:
            self.wait(eng, m)
        if fence:
            self.q[eng].append(lambda e, n=fence: e.nop(cycle_cnt=n))
        if sig is not None:
            val = sig.bump(k)
            self.trace[eng].append(("inc", id(sig), val, k))
            self.q[eng].append(lambda e, f=fn, s=sig, kk=k: f(e).then_inc(s.h, kk))
            return (sig, val)
        self.trace[eng].append(("op",))
        self.q[eng].append(lambda e, f=fn: f(e))
        return None


PHASES = int(os.environ.get("KMK_PHASES", "99"))
T1A = int(os.environ.get("KMK_T1A", "99"))
DUMP = int(os.environ.get("KMK_DUMP", "0"))
FLUSH = int(os.environ.get("KMK_FLUSH", "8"))
NOLO = int(os.environ.get("KMK_NOLO", "0"))
NOEPI = int(os.environ.get("KMK_NOEPI", "0"))


LAST_PR = [None]


def build_nc():
    nc = bass.Bass("TRN2", target_bir_lowering=False)

    d_xT = nc.dram_tensor("xT", [1024, TOK], F32, kind="ExternalInput").ap()
    d_wqkv = nc.dram_tensor("wqkv", [1024, 3072], F32, kind="ExternalInput").ap()
    d_bq = nc.dram_tensor("bq_pc", [P, 8], F32, kind="ExternalInput").ap()
    d_bkv = nc.dram_tensor("bkv_row", [1, 2048], F32, kind="ExternalInput").ap()
    d_wproj = nc.dram_tensor("wproj", [1024, 1024], F32, kind="ExternalInput").ap()
    d_bproj = nc.dram_tensor("bproj_pc", [P, 8], F32, kind="ExternalInput").ap()
    d_bwT = nc.dram_tensor("bwT", [1024, 1024], F32, kind="ExternalInput").ap()
    d_bb = nc.dram_tensor("bb_pc", [P, 8], F32, kind="ExternalInput").ap()
    d_afl = nc.dram_tensor("afl", [1024, 128], F32, kind="ExternalInput").ap()
    d_bfl = nc.dram_tensor("bfl", [128, 1024], F32, kind="ExternalInput").ap()
    d_rwT = nc.dram_tensor("rwT", [1024, 8], F32, kind="ExternalInput").ap()
    d_rb = nc.dram_tensor("rb_pc", [8, 1], F32, kind="ExternalInput").ap()
    d_y = nc.dram_tensor("y", [1024, TOK], F32, kind="ExternalOutput").ap()
    d_dbgz = nc.dram_tensor("dbgz", [16, TOK], F32, kind="ExternalOutput").ap() if DUMP else None
    d_dbgu = nc.dram_tensor("dbgu", [P, NCH * 1024], F32, kind="ExternalOutput").ap() if DUMP else None
    d_dbgwt = nc.dram_tensor("dbgwt", [P, 128], F32, kind="ExternalOutput").ap() if DUMP else None
    d_dbglt = nc.dram_tensor("dbglt", [P, 64], F32, kind="ExternalOutput").ap() if DUMP else None
    d_dbgls = nc.dram_tensor("dbgls", [8, 1024], F32, kind="ExternalOutput").ap() if DUMP else None
    d_dbgr2 = nc.dram_tensor("dbgr2", [P, NCH * 1024], F32, kind="ExternalOutput").ap() if DUMP else None
    d_dbgr1 = nc.dram_tensor("dbgr1", [P, NCH * 1024], F32, kind="ExternalOutput").ap() if DUMP else None
    d_dbgat = nc.dram_tensor("dbgat", [P, NCH * 1024], BF16, kind="ExternalOutput").ap() if DUMP else None
    d_ccin = nc.dram_tensor("ccin", [P, 520], F32, kind="Internal").ap()
    d_ccout = nc.dram_tensor("ccout", [P, 520], F32, kind="Internal").ap()

    es = ExitStack()
    sb = lambda name, shape, dt: es.enter_context(nc.sbuf_tensor(name, shape, dt))

    phiq = sb("phiq", [P, NCH * TOK], BF16)
    e_all = sb("e_all", [16, NCH * P], BF16)
    s_exp = sb("s_exp", [8, P], BF16)
    idn = sb("idn", [P, P], F32)
    iot = sb("iot", [P, P], F32)
    ones_bf = sb("ones_bf", [P, 1], BF16)
    ones_f = sb("ones_f", [P, 1], F32)
    onesr_bf = sb("onesr_bf", [1, P], BF16)
    onesr_f = sb("onesr_f", [1, P], F32)
    bq_pc = sb("s_bq_pc", [P, 8], F32)
    bproj_pc = sb("s_bproj_pc", [P, 8], F32)
    bb_pc = sb("s_bb_pc", [P, 8], F32)
    rb_pc = sb("s_rb_pc", [8, 1], F32)
    bkv_bf = sb("bkv_bf", [1, 2048], BF16)
    epsz = sb("epsz", [P, 1], F32)
    epsln = sb("epsln", [P, 1], F32)
    kvpart = sb("kvpart", [P, 520], F32)
    kvall_f = sb("kvall_f", [P, 520], F32)
    kvall_bf = sb("kvall_bf", [P, 520], BF16)
    kvall_lo = sb("kvall_lo", [P, 520], BF16)
    kvp = sb("kvp", [P, NCH * P], BF16)
    kvp_lo = sb("kvp_lo", [P, NCH * P], BF16)
    ksbd = sb("ksbd", [P, NCH * 16], BF16)
    ksbd_lo = sb("ksbd_lo", [P, NCH * 16], BF16)
    xsum = sb("xsum", [1, TOK], F32)
    wpbar_f = sb("wpbar_f", [P, 8], F32)
    wpbar2_f = sb("wpbar2_f", [P, 8], F32)
    wpbar_bf = sb("wpbar_bf", [P, 8], BF16)
    bwbar_f = sb("bwbar_f", [P, 8], F32)
    bwbar2_f = sb("bwbar2_f", [P, 8], F32)
    bwsum_f = sb("bwsum_f", [P, 8], F32)
    bwbar_bf = sb("bwbar_bf", [P, 8], BF16)
    bflbar_f = sb("bflbar_f", [P, 1], F32)
    bflbar_bf = sb("bflbar_bf", [P, 1], BF16)
    bsum_p = sb("bsum_p", [1, 1], F32)
    bsum_b = sb("bsum_b", [1, 1], F32)

    S = lambda name: Sem(es.enter_context(nc.semaphore(name)))
    sLDg, sLDs, sOUT = S("sldg"), S("slds"), S("sout")
    sT, sV, sSc, sG, sCC = S("st_"), S("sv_"), S("ssc"), S("sg_"), S("scc")

    pr = Prog()
    LAST_PR[0] = pr
    TEN, VEC, SCA, GPS, SYN = "tensor", "vector", "scalar", "gpsimd", "sync"
    M = {}

    def vflush(bit=1):
        if not (FLUSH & bit):
            return
        m = pr.op(VEC, lambda e: e.nop(), sig=sV)
        m2 = pr.op(GPS, lambda e: e.nop(), sig=sG, waits=(m,))
        pr.wait(VEC, m2)

    def dma(eng, out, in_, sem, waits=()):
        return pr.op(eng, lambda e, o=out, i=in_: e.dma_start(out=o, in_=i),
                     sig=sem, k=16, waits=waits)

    pk = [es.enter_context(nc.psum_tensor(f"pk{i}", [P, 512], F32)) for i in range(8)]
    bankm = [None] * 8

    def bank_use(i, new_marker):
        bankm[i] = new_marker

    es1 = ExitStack()
    sb1 = lambda name, shape, dt: es1.enter_context(nc.sbuf_tensor(name, shape, dt))
    x_bf = sb1("x_bf", [P, NCH * TOK], BF16)
    phik = sb1("phik", [P, NTT * 1024], BF16)
    vaug = sb1("vaug", [P, NTT * 1040], BF16)
    e_scr = sb1("e_scr", [P, 2 * 512], F32)
    r_scr = sb1("r_scr", [P, 2 * 512], F32)
    ia1 = e_scr[0:16, 0:NCH * P]
    ia2 = r_scr[0:8, 0:P]
    es1b = ExitStack()
    wkv_bf = es1b.enter_context(nc.sbuf_tensor("wkv_bf", [P, NCH * 2048], BF16))

    pr.op(GPS, lambda e: e.memset(
        vaug.rearrange("p (t k e) -> p t k e", t=NTT, k=16)[:, :, :, 64:65], 1.0))
    pr.op(GPS, lambda e: e.memset(ones_bf[:, :], 1.0))
    pr.op(GPS, lambda e: e.memset(ones_f[:, :], 1.0))
    pr.op(GPS, lambda e: e.memset(onesr_bf[:, :], 1.0))
    pr.op(GPS, lambda e: e.memset(onesr_f[:, :], 1.0))
    pr.op(GPS, lambda e: e.memset(epsz[:, :], EPS_Z))
    pr.op(GPS, lambda e: e.memset(epsln[:, :], EPS_LN))
    pr.op(GPS, lambda e: e.memset(kvp[:, :], 0.0))
    pr.op(GPS, lambda e: e.memset(kvp_lo[:, :], 0.0))
    pr.op(GPS, lambda e: e.memset(ksbd_lo[:, :], 0.0))
    pr.op(GPS, lambda e: e.memset(ksbd[:, :], 0.0))
    pr.op(GPS, lambda e: e.iota(iot[:, :], pattern=[[1, P]], base=0, channel_multiplier=-1,
                                allow_small_or_imprecise_dtypes=True))
    m_ia1 = pr.op(GPS, lambda e: e.iota(ia1.rearrange("p (c h i) -> p c h i", c=8, h=2),
                                        pattern=[[2, 8], [1, 2], [0, 64]], base=0,
                                        channel_multiplier=-1,
                                        allow_small_or_imprecise_dtypes=True), sig=sG)
    m_ia2 = pr.op(GPS, lambda e: e.iota(ia2.rearrange("p (c r) -> p c r", c=8),
                                        pattern=[[1, 8], [0, 16]], base=0,
                                        channel_multiplier=-1,
                                        allow_small_or_imprecise_dtypes=True), sig=sG)
    m_idn = pr.op(VEC, lambda e: e.tensor_scalar(idn[:, :], iot[:, :], 0.0, None, ALU.is_equal),
                  sig=sV, waits=(m_ia1,))
    m_eall = pr.op(VEC, lambda e: e.tensor_scalar(e_all[:, :], ia1[:, :], 0.0, None, ALU.is_equal),
                   sig=sV, waits=(m_ia1,))
    m_sexp = pr.op(VEC, lambda e: e.tensor_scalar(s_exp[:, :], ia2[:, :], 0.0, None, ALU.is_equal),
                   sig=sV, waits=(m_ia2,))

    mx = [dma(GPS, x_bf[:, c * TOK:(c + 1) * TOK], d_xT[c * P:(c + 1) * P, :], sLDg)
          for c in range(NCH)]
    mwkv = [dma(GPS, wkv_bf[:, c * 2048:(c + 1) * 2048],
                d_wqkv[c * P:(c + 1) * P, 1024:3072], sLDg) for c in range(NCH)]
    m_bkv = dma(GPS, bkv_bf[:, :], d_bkv[:, :], sLDg)
    mx2 = [[mx[c] for c in range(NCH)] for _ in range(4)]
    mwkv2 = [[mwkv[c] for c in range(NCH)] for _ in range(4)]
    m_bq = dma(SYN, bq_pc[:, :], d_bq[:, :], sLDs)
    m_bproj = dma(SYN, bproj_pc[:, :], d_bproj[:, :], sLDs)
    m_bb = dma(SYN, bb_pc[:, :], d_bb[:, :], sLDs)
    m_rb = dma(SYN, rb_pc[:, :], d_rb[:, :], sLDs)

    if PHASES <= 0:
        def _dbg0(last_marks, stacks):
            dma(SYN, d_y[0:P, 0:512], x_bf[:, 0:1024].bitcast(F32), sOUT,
                waits=tuple(last_marks))
            pr.op(SYN, lambda e: e.wait_ge(sOUT.h, 16))
            _mk_blocks(nc, pr)
            for st_ in stacks:
                st_.close()
            es.close()
            return nc
        return _dbg0([mx[NCH - 1], mwkv[NCH - 1], m_bkv, m_bq, m_rb, m_idn, m_eall,
                      m_sexp], [es1b, es1])

    phik_ready = [None] * NTT
    vaug_ready = [None] * NTT
    scrm = {}
    _ntt = min(NTT, T1A)
    for t in range(_ntt):
        bs = (t % 2) * 4
        w0 = [m_bkv] + [bankm[bs + i] for i in range(4)]
        for c in range(NCH):
            for kc in range(4):
                wmm = [mx2[t // 4][c], mwkv2[kc][c]]
                if c == 0 and kc == 0:
                    wmm += w0
                pr.op(TEN, lambda e, b=bs + kc, c=c, t=t, kc=kc: e.matmul(
                    pk[b][:, :],
                    x_bf[:, c * TOK + t * P: c * TOK + (t + 1) * P],
                    wkv_bf[:, c * 2048 + kc * 512: c * 2048 + (kc + 1) * 512],
                    start=(c == 0), stop=False),
                    waits=tuple(x for x in wmm if x))
        mm_done = [None] * 4
        for kc in range(4):
            mm_done[kc] = pr.op(TEN, lambda e, b=bs + kc, kc=kc: e.matmul(
                pk[b][:, :], onesr_bf[0:1, 0:P], bkv_bf[0:1, kc * 512:(kc + 1) * 512],
                start=False, stop=True), sig=sT)
        for kc in range(2):
            b = bs + kc
            scr = (2 * t + kc) % 2
            if NOEPI:
                mphi = pr.op(VEC, lambda e, b=b, t=t, kc=kc: e.tensor_copy(
                    phik[:, t * 1024 + kc * 512: t * 1024 + (kc + 1) * 512], pk[b][:, :]),
                    sig=sV, waits=(mm_done[kc],))
                bank_use(b, mphi)
                phik_ready[t] = mphi
                continue
            me = pr.op(SCA, lambda e, b=b, scr=scr: e.activation(
                e_scr[:, scr * 512:(scr + 1) * 512], pk[b][:, :], AF.Exp),
                sig=sSc, fence=180,
                waits=tuple(x for x in (mm_done[kc], scrm.get(("e", scr)), m_eall) if x))
            mr = pr.op(VEC, lambda e, b=b, scr=scr: e.tensor_scalar(
                r_scr[:, scr * 512:(scr + 1) * 512], pk[b][:, :], 0.0, None, ALU.max),
                sig=sV,
                waits=tuple(x for x in (me, scrm.get(("r", scr)), m_sexp) if x))
            mphi = pr.op(VEC, lambda e, t=t, kc=kc, scr=scr: e.scalar_tensor_tensor(
                phik[:, t * 1024 + kc * 512: t * 1024 + (kc + 1) * 512],
                e_scr[:, scr * 512:(scr + 1) * 512], 1.0,
                r_scr[:, scr * 512:(scr + 1) * 512], ALU.min, ALU.add),
                sig=sV, waits=(me,))
            scrm[("e", scr)] = mphi
            scrm[("r", scr)] = mphi
            bank_use(b, mphi)
            phik_ready[t] = mphi
        for kc in range(2, 4):
            b = bs + kc
            hv0 = (kc - 2) * 8
            if NOEPI:
                mv = pr.op(VEC, lambda e, b=b, t=t, kc=kc: e.tensor_copy(
                    vaug[:, t * 1040 + (kc - 2) * 512: t * 1040 + (kc - 1) * 512],
                    pk[b][:, :]), sig=sV, waits=(mm_done[kc],))
                bank_use(b, mv)
                vaug_ready[t] = mv
                continue
            mv = pr.op(SCA, lambda e, b=b, t=t, hv0=hv0: e.copy(
                vaug.rearrange("p (t k e) -> p t k e", t=NTT, k=16)[:, t, hv0:hv0 + 8, 0:64],
                pk[b][:, :].rearrange("p (k e) -> p k e", k=8)),
                sig=sSc, fence=180, waits=(mm_done[kc],))
            bank_use(b, mv)
            vaug_ready[t] = mv

    def _debug_finish(last_marks, stacks):
        dma(SYN, d_y[0:P, 0:512], phik[:, 0:1024].bitcast(F32), sOUT,
            waits=tuple(last_marks))
        pr.op(SYN, lambda e: e.wait_ge(sOUT.h, 16))
        _mk_blocks(nc, pr)
        for st_ in stacks:
            st_.close()
        es.close()
        return nc

    if PHASES <= 1:
        return _debug_finish([phik_ready[_ntt - 1], vaug_ready[_ntt - 1]], [es1b, es1])

    w0kv = [phik_ready[NTT - 1], vaug_ready[NTT - 1], bankm[6], bankm[7]]
    first = True
    for pair in range(8):
        bk = pk[6 + pair // 4]
        off = (pair % 4) * 128
        for par in range(2):
            h_ = 2 * pair + par
            for t in range(NTT):
                last = (pair == 7 and par == 1 and t == NTT - 1)
                m_kvmm = pr.op(TEN, lambda e, bk=bk, off=off, par=par, h_=h_, t=t: e.matmul(
                    bk[par * 64:(par + 1) * 64, off:off + 65],
                    phik[:, t * 1024 + h_ * 64: t * 1024 + (h_ + 1) * 64],
                    vaug[:, t * 1040 + h_ * 65: t * 1040 + (h_ + 1) * 65],
                    start=(t == 0), stop=(t == NTT - 1)),
                    sig=sT if last else None,
                    waits=tuple(x for x in w0kv if x) if first else ())
                first = False
    m_ev0 = pr.op(VEC, lambda e: e.tensor_copy(
        kvpart[:, 0:260].rearrange("p (c e) -> p c e", c=4),
        pk[6][:, :].rearrange("p (c e) -> p c e", c=4)[:, :, 0:65]),
        sig=sV, fence=180, waits=(m_kvmm,))
    m_ev1 = pr.op(VEC, lambda e: e.tensor_copy(
        kvpart[:, 260:520].rearrange("p (c e) -> p c e", c=4),
        pk[7][:, :].rearrange("p (c e) -> p c e", c=4)[:, :, 0:65]),
        sig=sV, fence=180, waits=(m_kvmm,))
    bank_use(6, m_ev0)
    bank_use(7, m_ev1)
    if PHASES <= 12:
        return _debug_finish([m_ev0, m_ev1], [es1b, es1])
    m_ccst = dma(SYN, d_ccin[:, :], kvpart[:, :], sLDs, waits=(m_ev0, m_ev1))
    if PHASES <= 13:
        return _debug_finish([m_ccst], [es1b, es1])
    m_cc = pr.op(GPS, lambda e: e.collective_compute(
        "AllReduce", ALU.add, ins=[d_ccin[:, :]], outs=[d_ccout[:, :]],
        replica_groups=[[0, 1, 2, 3], [4, 5, 6, 7]]), sig=sCC, waits=(m_ccst,))
    if PHASES <= 14:
        return _debug_finish([(sCC, sCC.n)], [es1b, es1])

    if PHASES <= 15:
        m_kvld0 = pr.op(GPS, lambda e: e.dma_start(out=kvpart[:, :], in_=d_ccout[:, :]),
                        sig=sLDg, k=16, waits=(m_cc,))
        return _debug_finish([m_kvld0], [es1b, es1])
    if PHASES <= 16:
        m_kvld0 = pr.op(GPS, lambda e: e.dma_start(out=kvall_bf[:, :], in_=d_ccout[:, :]),
                        sig=sLDg, k=16, waits=(m_cc,))
        return _debug_finish([m_kvld0], [es1b, es1])

    es1b.close()
    es2 = ExitStack()
    wq_bf = es2.enter_context(nc.sbuf_tensor("wq_bf", [P, NCH * 1024], BF16))
    mwq = [None] * NCH
    for c in range(NCH):
        mwq[c] = dma(GPS, wq_bf[:, c * 1024:(c + 1) * 1024],
                     d_wqkv[c * P:(c + 1) * P, 0:1024], sLDg,
                     waits=(m_kvmm,) if c == 0 else ())
    mwq2 = [[mwq[c] for c in range(NCH)] for _ in range(2)]

    phiq_ready = [[None] * NT4 for _ in range(NCH)]
    xsum_ready = [None] * NT4
    for t4 in range(NT4):
        wx = ([bankm[4]] if t4 == 0 else [xsum_ready[t4 - 1]]) + [mx2[t4][c] for c in range(NCH)]
        for c in range(NCH):
            m_xs = pr.op(TEN, lambda e, c=c, t4=t4: e.matmul(
                pk[4][0:1, :], ones_bf[:, 0:1],
                x_bf[:, c * TOK + t4 * 512: c * TOK + (t4 + 1) * 512],
                start=(c == 0), stop=(c == NCH - 1)),
                sig=sT if c == NCH - 1 else None,
                waits=tuple(x for x in wx if x) if c == 0 else ())
        xsum_ready[t4] = pr.op(VEC, lambda e, t4=t4: e.tensor_copy(
            xsum[:, t4 * 512:(t4 + 1) * 512], pk[4][0:1, :]), sig=sV, fence=180, waits=(m_xs,))
        bank_use(4, xsum_ready[t4])
        for qc in range(NCH):
            g = t4 * NCH + qc
            b = g % 4
            wq_w = list(mwq2[qc // 4]) + ([bankm[b]] if bankm[b] else [])
            for c in range(NCH):
                m_q = pr.op(TEN, lambda e, b=b, c=c, qc=qc, t4=t4: e.matmul(
                    pk[b][:, :],
                    wq_bf[:, c * 1024 + qc * P: c * 1024 + (qc + 1) * P],
                    x_bf[:, c * TOK + t4 * 512: c * TOK + (t4 + 1) * 512],
                    start=(c == 0), stop=(c == NCH - 1)),
                    sig=sT if c == NCH - 1 else None,
                    waits=tuple(wq_w) if c == 0 else ())
            scr = g % 2
            me = pr.op(SCA, lambda e, b=b, qc=qc, scr=scr: e.activation(
                e_scr[:, scr * 512:(scr + 1) * 512], pk[b][:, :], AF.Exp,
                bias=bq_pc[:, qc:qc + 1]),
                sig=sSc, fence=180,
                waits=tuple(x for x in (m_q, m_bq, scrm.get(("e", scr))) if x))
            mr = pr.op(VEC, lambda e, b=b, qc=qc, scr=scr: e.tensor_scalar(
                r_scr[:, scr * 512:(scr + 1) * 512], pk[b][:, :],
                bq_pc[:, qc:qc + 1], 0.0, ALU.add, ALU.max),
                sig=sV, waits=tuple(x for x in (me, scrm.get(("r", scr))) if x))
            mphi = pr.op(VEC, lambda e, qc=qc, t4=t4, scr=scr: e.scalar_tensor_tensor(
                phiq[:, qc * TOK + t4 * 512: qc * TOK + (t4 + 1) * 512],
                e_scr[:, scr * 512:(scr + 1) * 512], 1.0,
                r_scr[:, scr * 512:(scr + 1) * 512], ALU.min, ALU.add),
                sig=sV, waits=(me,))
            scrm[("e", scr)] = mphi
            scrm[("r", scr)] = mphi
            bank_use(b, mphi)
            phiq_ready[qc][t4] = mphi

    m_phiq_all = phiq_ready[NCH - 1][NT4 - 1]
    m_scal_guard = me
    m_ten_guard = m_q

    if PHASES <= 2:
        return _debug_finish([m_phiq_all, m_ev0, m_ev1], [es2, es1])

    m_bs1 = pr.op(TEN, lambda e: e.matmul(pk[5][0:1, 0:8], ones_f[:, 0:1], bproj_pc[:, :],
                                          start=True, stop=True),
                  sig=sT, waits=tuple(x for x in (m_bproj, bankm[5]) if x))
    m_bs2 = pr.op(TEN, lambda e: e.matmul(pk[5][0:1, 8:16], ones_f[:, 0:1], bb_pc[:, :],
                                          start=True, stop=True), sig=sT, waits=(m_bb,))
    m_bsv1 = pr.op(VEC, lambda e: e.tensor_reduce(bsum_p[:, :], pk[5][0:1, 0:8], AX.X, ALU.add),
                   sig=sV, fence=180, waits=(m_bs1, m_bs2))
    m_bsv2 = pr.op(VEC, lambda e: e.tensor_reduce(bsum_b[:, :], pk[5][0:1, 8:16], AX.X, ALU.add),
                   sig=sV)
    bank_use(5, m_bsv2)

    m_kvld = pr.op(GPS, lambda e: e.dma_start(out=kvall_f[:, :], in_=d_ccout[:, :]),
                   sig=sLDg, k=16, waits=(m_cc,))
    m_kvldb = pr.op(GPS, lambda e: e.dma_start(out=kvall_bf[:, :], in_=d_ccout[:, :]),
                    sig=sLDg, k=16)
    m_lo = pr.op(VEC, lambda e: e.scalar_tensor_tensor(
        kvall_lo[:, :], kvall_bf[:, :], -1.0, kvall_f[:, :], ALU.mult, ALU.add),
        sig=sV, waits=(m_kvld, m_kvldb))
    m_ks = []
    for src_t, kvp_t, ksbd_t in (((kvall_bf, kvp, ksbd), (kvall_lo, kvp_lo, ksbd_lo))[:1 if NOLO else 2]):
        kvr = src_t.rearrange("p (c e) -> p c e", c=8)
        kvpr = kvp_t.rearrange("p (c e) -> p c e", c=8)
        m_ks.append(pr.op(VEC, lambda e, kvpr=kvpr, kvr=kvr: e.tensor_copy(
            kvpr[0:64, :, 0:64], kvr[0:64, :, 0:64]), sig=sV, waits=(m_lo,)))
        m_ks.append(pr.op(VEC, lambda e, kvpr=kvpr, kvr=kvr: e.tensor_copy(
            kvpr[64:128, :, 64:128], kvr[64:128, :, 0:64]), sig=sV))
        for c8 in range(8):
            m_ks.append(pr.op(VEC, lambda e, c8=c8, ksbd_t=ksbd_t, kvr=kvr: e.tensor_copy(
                ksbd_t[0:64, c8 * 16 + 2 * c8: c8 * 16 + 2 * c8 + 1],
                kvr[0:64, c8, 64:65]), sig=sV))
            m_ks.append(pr.op(VEC, lambda e, c8=c8, ksbd_t=ksbd_t, kvr=kvr: e.tensor_copy(
                ksbd_t[64:128, c8 * 16 + 2 * c8 + 1: c8 * 16 + 2 * c8 + 2],
                kvr[64:128, c8, 64:65]), sig=sV))
    m_kvready = m_ks[-1]

    if PHASES <= 3:
        return _debug_finish([m_kvready, m_bsv2], [es2, es1])

    es2.close()
    es1.close()
    es4 = ExitStack()
    sb4 = lambda name, shape, dt: es4.enter_context(nc.sbuf_tensor(name, shape, dt))
    A = sb4("A", [P, NCH * 1024], F32)
    B = sb4("B", [P, NCH * 1024], F32)
    wproj_bf = sb4("wproj_bf", [P, NCH * 512], BF16)
    bwT_bf = sb4("bwT_bf", [P, NCH * 512], BF16)
    afl_bf = sb4("afl_bf", [P, NCH * P], BF16)
    bfl_bf = sb4("bfl_bf", [P, 1024], BF16)
    rwT_bf = sb4("rwT_bf", [P, NCH * 8], BF16)
    tsc = sb4("tsc", [P, 512], F32)
    ysc = sb4("ysc", [P, 2 * 512], F32)
    zbsc = sb4("zbsc", [P, 512], F32)
    sqsc = sb4("sqsc", [P, 8 * 512], BF16)
    wexp_sb = sb4("wexp_sb", [P, 512], F32)
    attn_bf = sb4("attn_bf", [P, NCH * 1024], BF16)
    u_bf = attn_bf
    wax_bf = sb4("wax_bf", [P, 1024], BF16)
    zrow = sb4("zrow", [16, 512], F32)
    zexp = sb4("zexp", [16, 512], F32)
    zhi = sb4("zhi", [16, 512], BF16)
    zlo = sb4("zlo", [16, 512], BF16)
    ltm = sb4("ltm", [P, 64], F32)
    wtil = sb4("wtil", [P, 64], F32)
    wtil2 = sb4("wtil2", [P, 64], F32)
    wT_bf = sb4("wT_bf", [8, 1024], BF16)
    lsc = sb4("lsc", [8, 512], F32)
    mrow = sb4("mrow", [1, 512], F32)
    mrow2 = sb4("mrow2", [1, 512], F32)
    msrow = sb4("msrow", [1, 512], F32)
    vrow = sb4("vrow", [1, 512], F32)
    srow = sb4("srow", [1, 512], F32)
    srow2 = sb4("srow2", [1, 512], F32)
    r16a = sb4("r16a", [P, 8], F32)
    r16b = sb4("r16b", [P, 8], F32)
    r16c = sb4("r16c", [P, 8], F32)
    r16d = sb4("r16d", [P, 8], F32)
    r16e = sb4("r16e", [P, 8], F32)
    twos8 = sb4("twos8", [P, 8], F32)
    r64a = sb4("r64a", [P, 64], F32)
    r64b = sb4("r64b", [P, 64], F32)
    r64c = sb4("r64c", [P, 64], F32)
    r16f = sb4("r16f", [P, 8], F32)
    r16g = sb4("r16g", [P, 8], F32)
    r16h = sb4("r16h", [P, 8], F32)
    r16i = sb4("r16i", [P, 8], F32)
    r16j = sb4("r16j", [P, 8], F32)

    ph4_guard = (m_ten_guard, m_phiq_all, m_scal_guard)
    m_ph4first = dma(GPS, afl_bf.rearrange("p (c r) -> p c r", c=8),
                     d_afl.rearrange("(c p) r -> p c r", p=P), sLDg, waits=ph4_guard)
    mafl = m_ph4first
    mbfl = dma(GPS, bfl_bf[:, :], d_bfl[:, :], sLDg)
    mrwT = dma(GPS, rwT_bf.rearrange("p (c r) -> p c r", c=8),
                   d_rwT.rearrange("(c p) r -> p c r", p=P), sLDg)
    m_tw8 = pr.op(GPS, lambda e: e.memset(twos8[:, :], 2.0), sig=sG)
    m_bflb = pr.op(VEC, lambda e: e.tensor_reduce(bflbar_f[:, :], bfl_bf[:, :], AX.X, ALU.add),
                   sig=sV, waits=(mbfl,))
    m_bflbc = pr.op(SCA, lambda e: e.copy(bflbar_bf[:, :], bflbar_f[:, :]),
                    sig=sSc, waits=(m_bflb,))

    y_total = 0
    for h in range(2):
        mA = [None] * NCH
        for c in range(NCH):
            w = list(ph4_guard) if (h == 0 and c == 0) else []
            if h == 1:
                w.append(M.get(("Afree", c)))
                w.append(M.get(("dumpU",)))
            mA[c] = dma(SYN, A[:, c * 1024:(c + 1) * 1024],
                        d_xT[c * P:(c + 1) * P, h * 1024:(h + 1) * 1024], sLDs,
                        waits=tuple(x for x in w if x))

        attn_ready = [[None] * NCH, [None] * NCH]
        r1_ready = [[None] * NCH, [None] * NCH]
        for lt4 in range(2):
            t4 = 2 * h + lt4
            wD = [m_kvready] + [phiq_ready[c][t4] for c in range(NCH)] + [bankm[0]]
            for c in range(NCH):
                for kt, ksbd_t in ((0, ksbd), (1, ksbd_lo)):
                    last = (c == NCH - 1 and kt == 1)
                    m_D = pr.op(TEN, lambda e, c=c, t4=t4, ksbd_t=ksbd_t, kt=kt, last=last:
                        e.matmul(
                            pk[0][0:16, :], ksbd_t[:, c * 16:(c + 1) * 16],
                            phiq[:, c * TOK + t4 * 512: c * TOK + (t4 + 1) * 512],
                            start=(c == 0 and kt == 0), stop=last),
                        sig=sT if last else None,
                        waits=tuple(x for x in wD if x) if (c == 0 and kt == 0) else ())
            m_ln = pr.op(SCA, lambda e: e.activation(zrow[0:16, :], pk[0][0:16, :], AF.Ln,
                                                     bias=epsz[0:16, 0:1]),
                         sig=sSc, fence=180,
                         waits=tuple(x for x in (m_D, M.get(("zfree",)), M.get(("zfree2",))) if x))
            bank_use(0, m_ln)
            m_neg = pr.op(VEC, lambda e: e.tensor_scalar_mul(zexp[0:16, :], zrow[0:16, :], -1.0),
                          sig=sV, waits=(m_ln,))
            m_z = pr.op(SCA, lambda e: e.activation(zrow[0:16, :], zexp[0:16, :], AF.Exp),
                        sig=sSc, waits=(m_neg,))
            if DUMP:
                m_dz = dma(SYN, d_dbgz[:, t4 * 512:(t4 + 1) * 512], zrow[0:16, :], sOUT,
                           waits=(m_z,))
                M[("zfree2",)] = m_dz
            m_zh = pr.op(VEC, lambda e: e.tensor_copy(zhi[0:16, :], zrow[0:16, :]),
                         sig=sV, waits=tuple(x for x in (m_z, M.get(("zhir",))) if x))
            vflush(1)
            m_zl = pr.op(VEC, lambda e: e.scalar_tensor_tensor(
                zlo[0:16, :], zhi[0:16, :], -1.0, zrow[0:16, :], ALU.mult, ALU.add),
                sig=sV)
            M[("zfree",)] = m_zl

            for c8 in range(NCH):
                ba, bz = 1 + c8 % 2, 3 + c8 % 2
                wA_ = [m_kvready, phiq_ready[c8][t4], bankm[ba]]
                pr.op(TEN, lambda e, ba=ba, c8=c8, t4=t4: e.matmul(
                    pk[ba][:, :], kvp[:, c8 * P:(c8 + 1) * P],
                    phiq[:, c8 * TOK + t4 * 512: c8 * TOK + (t4 + 1) * 512],
                    start=True, stop=False),
                    waits=tuple(x for x in wA_ if x))
                m_amm = pr.op(TEN, lambda e, ba=ba, c8=c8, t4=t4: e.matmul(
                    pk[ba][:, :], kvp_lo[:, c8 * P:(c8 + 1) * P],
                    phiq[:, c8 * TOK + t4 * 512: c8 * TOK + (t4 + 1) * 512],
                    start=False, stop=True), sig=sT)
                wZ = [m_zh, m_zl, bankm[bz]]
                pr.op(TEN, lambda e, bz=bz, c8=c8: e.matmul(
                    pk[bz][:, :], e_all[:, c8 * P:(c8 + 1) * P], zhi[0:16, :],
                    start=True, stop=False), waits=tuple(x for x in wZ if x))
                m_zmm = pr.op(TEN, lambda e, bz=bz, c8=c8: e.matmul(
                    pk[bz][:, :], e_all[:, c8 * P:(c8 + 1) * P], zlo[0:16, :],
                    start=False, stop=True), sig=sT)
                scr = c8 % 2
                m_zev = pr.op(VEC, lambda e, bz=bz, scr=scr: e.tensor_copy(
                    zbsc[:, 0:512], pk[bz][:, :]),
                    sig=sV, fence=180,
                    waits=tuple(x for x in (m_zmm, scrm.get(("zb", scr))) if x))
                bank_use(bz, m_zev)
                wab = [m_amm, m_zev, M.get(("attnw", c8, lt4))]
                m_ab = pr.op(VEC, lambda e, ba=ba, c8=c8, lt4=lt4, scr=scr: e.tensor_tensor(
                    attn_bf[:, c8 * 1024 + lt4 * 512: c8 * 1024 + (lt4 + 1) * 512],
                    pk[ba][:, :], zbsc[:, 0:512], ALU.mult),
                    sig=sV, fence=180, waits=tuple(x for x in wab if x))
                scrm[("zb", scr)] = m_ab
                bank_use(ba, m_ab)
                attn_ready[lt4][c8] = m_ab
                if c8 == NCH - 1:
                    M[("zhir",)] = m_zmm


        for ochalf in range(2):
            mwp = [None] * NCH
            for c in range(NCH):
                wld = [M.get(("wprojw",))] if c == 0 else []
                mwp[c] = dma(GPS, wproj_bf[:, c * 512:(c + 1) * 512],
                             d_wproj[c * P:(c + 1) * P, ochalf * 512:(ochalf + 1) * 512],
                             sLDg, waits=tuple(x for x in wld if x))
            if h == 0:
                tgt = wpbar_f if ochalf == 0 else wpbar2_f
                for c in range(NCH):
                    _mw = pr.op(VEC, lambda e, c=c, tgt=tgt: e.tensor_reduce(
                        tgt[:, c:c + 1], wproj_bf[:, c * 512:(c + 1) * 512], AX.X, ALU.add),
                        sig=sV if c == NCH - 1 else None, waits=(mwp[c],))
                M[("wpred", ochalf)] = _mw
            for lt4 in range(2):
                for ocl in range(4):
                    oc = ochalf * 4 + ocl
                    bp = 5 + (lt4 * 4 + ocl) % 2
                    wp = [attn_ready[lt4][c] for c in range(NCH)] + \
                        [mwp[c] for c in range(NCH)] + [bankm[bp]]
                    for c in range(NCH):
                        m_pj = pr.op(TEN, lambda e, bp=bp, c=c, ocl=ocl, lt4=lt4: e.matmul(
                            pk[bp][:, :],
                            wproj_bf[:, c * 512 + ocl * P: c * 512 + (ocl + 1) * P],
                            attn_bf[:, c * 1024 + lt4 * 512: c * 1024 + (lt4 + 1) * 512],
                            start=(c == 0), stop=(c == NCH - 1)),
                            sig=sT if c == NCH - 1 else None,
                            waits=tuple(x for x in wp if x) if c == 0 else ())
                    wr1 = [m_pj, mA[oc], m_bproj, M.get(("r1guard", oc, lt4)),
                           M.get(("dumpR2",))]
                    m_r1 = pr.op(VEC, lambda e, bp=bp, oc=oc, lt4=lt4: e.scalar_tensor_tensor(
                        B[:, oc * 1024 + lt4 * 512: oc * 1024 + (lt4 + 1) * 512],
                        pk[bp][:, :], bproj_pc[:, oc:oc + 1],
                        A[:, oc * 1024 + lt4 * 512: oc * 1024 + (lt4 + 1) * 512],
                        ALU.add, ALU.add),
                        sig=sV, fence=180, waits=tuple(x for x in wr1 if x))
                    bank_use(bp, m_r1)
                    r1_ready[lt4][oc] = m_r1
            M[("wprojw",)] = m_pj
        if h == 0 and DUMP:
            m_dr1 = dma(SYN, d_dbgr1[:, :], B[:, :], sOUT,
                        waits=tuple(r1_ready[l][c] for l in range(2) for c in range(NCH)))
            m_dat = dma(SYN, d_dbgat[:, :], attn_bf[:, :], sOUT,
                        waits=tuple(attn_ready[l][c] for l in range(2) for c in range(NCH)))
            M[("dumpB",)] = m_dr1
            M[("dumpA",)] = m_dat
        if h == 0:
            M[("wpsum",)] = pr.op(VEC, lambda e: e.tensor_tensor(
                bwsum_f[:, :], wpbar_f[:, :], wpbar2_f[:, :], ALU.add),
                sig=sV, waits=(M[("wpred", 0)], M[("wpred", 1)]))
            m_wpbc = pr.op(SCA, lambda e: e.copy(wpbar_bf[:, :], bwsum_f[:, :]),
                           sig=sSc, waits=(M[("wpsum",)],))
            M[("wpbar",)] = m_wpbc

        u_ready = [[None] * NCH, [None] * NCH]
        for lt4 in range(2):
            t4 = 2 * h + lt4
            wm = [attn_ready[lt4][c] for c in range(NCH)] + [M[("wpbar",)], bankm[7]]
            for c in range(NCH):
                m_pm = pr.op(TEN, lambda e, c=c, lt4=lt4: e.matmul(
                    pk[7][0:1, :], wpbar_bf[:, c:c + 1],
                    attn_bf[:, c * 1024 + lt4 * 512: c * 1024 + (lt4 + 1) * 512],
                    start=(c == 0), stop=(c == NCH - 1)),
                    sig=sT if c == NCH - 1 else None,
                    waits=tuple(x for x in wm if x) if c == 0 else ())
            for c in range(NCH):
                M[("attnw", c, lt4)] = m_pm
            m_m = pr.op(VEC, lambda e, t4=t4: e.scalar_tensor_tensor(
                mrow[:, :], pk[7][0:1, :], bsum_p[0:1, 0:1],
                xsum[:, t4 * 512:(t4 + 1) * 512], ALU.add, ALU.add),
                sig=sV, fence=180,
                waits=tuple(x for x in (m_pm, xsum_ready[t4], m_bsv1,
                                        M.get(("vrowr",))) if x))
            bank_use(7, m_m)
            vflush(2)
            m_sq = [None] * NCH
            for c in range(NCH):
                m_sq[c] = pr.op(SCA, lambda e, c=c, lt4=lt4: e.activation(
                    sqsc[:, c * 512:(c + 1) * 512],
                    B[:, c * 1024 + lt4 * 512: c * 1024 + (lt4 + 1) * 512], AF.Square),
                    sig=sSc,
                    waits=tuple(x for x in (r1_ready[lt4][c], scrm.get(("sq", c))) if x))
                M[("r2guard", c, lt4)] = m_sq[c]
            for c in range(NCH):
                mmv = pr.op(TEN, lambda e, c=c: e.matmul(
                    pk[7][0:1, :], ones_bf[:, 0:1], sqsc[:, c * 512:(c + 1) * 512],
                    start=(c == 0), stop=(c == NCH - 1)),
                    sig=sT,
                    waits=tuple(x for x in (m_sq[c], bankm[7] if c == 0 else None) if x))
                scrm[("sq", c)] = mmv
            m_sv = mmv
            m_msq = pr.op(SCA, lambda e: e.activation(
                msrow[:, :], mrow[:, :], AF.Square, scale=1.0 / 1024.0), sig=sSc,
                waits=tuple(x for x in (m_m, M.get(("msrowr",))) if x))
            m_v = pr.op(VEC, lambda e: e.scalar_tensor_tensor(
                vrow[:, :], pk[7][0:1, :], 1.0 / 1024.0, msrow[:, :], ALU.mult, ALU.subtract),
                sig=sV, fence=180,
                waits=tuple(x for x in (m_sv, m_msq, M.get(("vrowr",))) if x))
            bank_use(7, m_v)
            m_l = pr.op(SCA, lambda e: e.activation(srow[:, :], vrow[:, :], AF.Ln,
                                                    bias=epsln[0:1, 0:1]),
                        sig=sSc, waits=tuple(x for x in (m_v, M.get(("srowr",))) if x))
            M[("vrowr",)] = m_l
            m_nh = pr.op(VEC, lambda e: e.tensor_scalar_mul(srow2[:, :], srow[:, :], -0.5),
                         sig=sV, waits=(m_l,))
            m_s = pr.op(SCA, lambda e: e.activation(srow[:, :], srow2[:, :], AF.Exp),
                        sig=sSc, waits=(m_nh,))
            m_ms = pr.op(VEC, lambda e: e.scalar_tensor_tensor(
                msrow[:, :], mrow[:, :], 1.0 / 1024.0, srow[:, :], ALU.mult, ALU.mult),
                sig=sV, waits=tuple(x for x in (m_s, M.get(("msrowr",))) if x))
            m_bm = pr.op(TEN, lambda e: e.matmul(
                pk[1][:, :], onesr_f[0:1, 0:P], msrow[0:1, :], start=True, stop=True),
                sig=sT, waits=tuple(x for x in (m_ms, bankm[1]) if x))
            m_bs = pr.op(TEN, lambda e: e.matmul(
                pk[2][:, :], onesr_f[0:1, 0:P], srow[0:1, :], start=True, stop=True),
                sig=sT, waits=tuple(x for x in (bankm[2],) if x))
            M[("srowr",)] = m_bs
            M[("msrowr",)] = m_bm
            lastp2 = None
            for c in range(NCH):
                scr = c % 2
                wap = [m_bs, m_bm, r1_ready[lt4][c], scrm.get(("ts", scr))]
                m_p1 = pr.op(VEC, lambda e, c=c, lt4=lt4, scr=scr: e.tensor_tensor(
                    tsc[:, 0:512],
                    B[:, c * 1024 + lt4 * 512: c * 1024 + (lt4 + 1) * 512],
                    pk[2][:, :], ALU.mult), sig=sV, waits=tuple(x for x in wap if x))
                m_p2 = pr.op(VEC, lambda e, c=c, lt4=lt4, scr=scr: e.tensor_tensor(
                    A[:, c * 1024 + lt4 * 512: c * 1024 + (lt4 + 1) * 512],
                    tsc[:, 0:512], pk[1][:, :], ALU.subtract),
                    sig=sV)
                scrm[("ts", scr)] = m_p2
                wub = [m_p2, M.get(("ubfw", c, lt4)), M.get(("dumpA",))]
                m_ub = pr.op(SCA, lambda e, c=c, lt4=lt4: e.copy(
                    u_bf[:, c * 1024 + lt4 * 512: c * 1024 + (lt4 + 1) * 512],
                    A[:, c * 1024 + lt4 * 512: c * 1024 + (lt4 + 1) * 512]),
                    sig=sSc, waits=tuple(x for x in wub if x))
                u_ready[lt4][c] = m_ub
                lastp2 = m_p2
            bank_use(1, lastp2)
            bank_use(2, lastp2)

        if h == 0 and DUMP:
            m_du = dma(SYN, d_dbgu[:, :], A[:, :], sOUT,
                       waits=tuple(u_ready[l][c] for l in range(2) for c in range(NCH)))
            M[("dumpU",)] = m_du
        m_lt = [None, None]
        for lt4 in range(2):
            t4 = 2 * h + lt4
            wl = [u_ready[lt4][c] for c in range(NCH)] + [mrwT, m_rb, bankm[0]]
            for c in range(NCH):
                m_lg = pr.op(TEN, lambda e, c=c, lt4=lt4: e.matmul(
                    pk[0][0:8, :], rwT_bf[:, c * 8:(c + 1) * 8],
                    u_bf[:, c * 1024 + lt4 * 512: c * 1024 + (lt4 + 1) * 512],
                    start=(c == 0), stop=(c == NCH - 1)),
                    sig=sT if c == NCH - 1 else None,
                    waits=tuple(x for x in wl if x) if c == 0 else ())
            m_lsb = pr.op(VEC, lambda e: e.tensor_scalar(
                lsc[0:8, :], pk[0][0:8, :], rb_pc[0:8, 0:1], None, ALU.add),
                sig=sV, fence=180,
                waits=tuple(x for x in (m_lg, M.get(("lsc",)), M.get(("lsc2",))) if x))
            bank_use(0, m_lsb)
            if h == 0 and DUMP:
                m_dls = dma(SYN, d_dbgls[:, lt4 * 512:(lt4 + 1) * 512], lsc[0:8, :], sOUT,
                            waits=(m_lsb,))
                M[("lsc2",)] = m_dls
            wtp = [m_lsb, m_idn, bankm[3]]
            for tb in range(4):
                m_tp = pr.op(TEN, lambda e, tb=tb: e.transpose(
                    pk[3][:, tb * 8:(tb + 1) * 8], lsc[0:8, tb * P:(tb + 1) * P],
                    idn[0:8, 0:8]), sig=sT if tb == 3 else None,
                    waits=tuple(x for x in wtp if x) if tb == 0 else ())
            M[("lsc",)] = m_tp
            m_lt[lt4] = pr.op(VEC, lambda e, lt4=lt4: e.tensor_copy(
                ltm[:, lt4 * 32:(lt4 + 1) * 32], pk[3][:, 0:32]), sig=sV, fence=180,
                waits=tuple(x for x in (m_tp, M.get(("ltmr",)), M.get(("ltm2",))) if x))
            bank_use(3, m_lt[lt4])

        if h == 0 and DUMP:
            m_dlt = dma(SYN, d_dbglt[:, :], ltm[:, :], sOUT, waits=(m_lt[0], m_lt[1]))
            M[("ltm2",)] = m_dlt
        l3 = ltm.rearrange("p (b e) -> p b e", b=8)
        w3a = r64a.rearrange("p (b e) -> p b e", b=8)
        w3b = r64b.rearrange("p (b e) -> p b e", b=8)
        wt2 = wtil2.rearrange("p (b e) -> p b e", b=8)
        pr.op(VEC, lambda e: e.tensor_reduce(r16a[:, :], l3, AX.X, ALU.max),
              waits=(m_lt[0], m_lt[1]))
        pr.op(VEC, lambda e: e.tensor_tensor(
            w3a, l3, r16a[:, :].unsqueeze(2).broadcast_to([P, 8, 8]), ALU.is_ge))
        pr.op(VEC, lambda e: e.tensor_reduce(r16b[:, :], w3a, AX.X, ALU.add))
        pr.op(VEC, lambda e: e.scalar_tensor_tensor(w3b, w3a, 1e9, l3,
                                                    ALU.mult, ALU.subtract))
        pr.op(VEC, lambda e: e.tensor_reduce(r16c[:, :], w3b, AX.X, ALU.min))
        vflush(8)
        pr.op(VEC, lambda e: e.tensor_tensor(r16e[:, :], r16b[:, :], twos8[:, :], ALU.is_ge),
              waits=(m_tw8,))
        pr.op(VEC, lambda e: e.tensor_tensor(r16d[:, :], r16a[:, :], r16c[:, :], ALU.add))
        vflush(8)
        pr.op(VEC, lambda e: e.tensor_tensor(r16f[:, :], r16e[:, :], r16d[:, :], ALU.mult))
        vflush(8)
        pr.op(VEC, lambda e: e.tensor_tensor(r16g[:, :], r16f[:, :], r16c[:, :], ALU.subtract))
        vflush(8)
        pr.op(VEC, lambda e: e.tensor_tensor(
            w3a, l3, r16g[:, :].unsqueeze(2).broadcast_to([P, 8, 8]), ALU.is_ge))
        m_rexp = pr.op(SCA, lambda e: e.activation(r64c[:, :], ltm[:, :], AF.Exp),
                       sig=sSc, waits=(m_lt[0], m_lt[1]))
        M[("ltmr",)] = m_rexp
        vflush(8)
        pr.op(VEC, lambda e: e.tensor_tensor(wtil2[:, :], r64c[:, :], r64a[:, :], ALU.mult),
              waits=(m_rexp,))
        vflush(8)
        m_den = pr.op(VEC, lambda e: e.tensor_reduce(r16h[:, :], wt2, AX.X, ALU.add),
                      sig=sV)
        vflush(8)
        m_dl = pr.op(SCA, lambda e: e.activation(r16i[:, :], r16h[:, :], AF.Ln),
                     sig=sSc, waits=(m_den,))
        m_ng = pr.op(VEC, lambda e: e.tensor_scalar_mul(r16j[:, :], r16i[:, :], -1.0),
                     sig=sV, waits=(m_dl,))
        m_di = pr.op(SCA, lambda e: e.activation(r16i[:, :], r16j[:, :], AF.Exp),
                     sig=sSc, waits=(m_ng,))
        wtw = [M.get(("wtilr",)), M.get(("dumpWT",)), m_di]
        m_wtil = pr.op(VEC, lambda e: e.scalar_tensor_tensor(
            wtil[:, :].rearrange("p (b e) -> p b e", b=8), wt2, SCALING,
            r16i[:, :].unsqueeze(2).broadcast_to([P, 8, 8]), ALU.mult, ALU.mult),
            sig=sV, waits=tuple(x for x in wtw if x))
        if h == 0 and DUMP:
            m_dx = dma(SYN, d_dbgwt[:, 64:72], r16g[:, :], sOUT, waits=(m_wtil,))
            m_dx = dma(SYN, d_dbgwt[:, 72:80], r16h[:, :], sOUT)
            m_dx = dma(SYN, d_dbgwt[:, 80:88], r16i[:, :], sOUT)
            m_dwn = dma(SYN, d_dbgr2[:, 0:64], wtil2[:, :], sOUT)
            m_dex = dma(SYN, d_dbgr2[:, 128:192], r64c[:, :], sOUT)
            M[("dbgwun",)] = m_dex
        for tb in range(8):
            bz = 3 + tb % 2
            wtt = [m_wtil, bankm[bz], M.get(("wTw",))]
            mt = pr.op(TEN, lambda e, tb=tb, bz=bz: e.transpose(
                pk[bz][0:8, 0:P], wtil[:, tb * 8:(tb + 1) * 8], idn[:, :]),
                sig=sT, waits=tuple(x for x in wtt if x))
            mwt = pr.op(VEC, lambda e, tb=tb, bz=bz: e.tensor_copy(
                wT_bf[0:8, tb * P:(tb + 1) * P], pk[bz][0:8, 0:P]), sig=sV, fence=180,
                waits=(mt,))
            bank_use(bz, mwt)
            M[("wtilr",)] = mt
        m_wT = mwt

        wax_ready = [None, None]
        for lt4 in range(2):
            t4 = 2 * h + lt4
            bz = 3 + lt4
            ww = [m_wT, m_sexp, bankm[bz]]
            m_we = pr.op(TEN, lambda e, bz=bz, lt4=lt4: e.matmul(
                pk[bz][:, :], s_exp[0:8, :], wT_bf[0:8, lt4 * 512:(lt4 + 1) * 512],
                start=True, stop=True), sig=sT, waits=tuple(x for x in ww if x))
            M[("wTw",)] = m_we
            m_wev = pr.op(VEC, lambda e, bz=bz, lt4=lt4: e.tensor_copy(
                wexp_sb[:, 0:512], pk[bz][:, :]), sig=sV, fence=180, waits=(m_we,))
            bank_use(bz, m_wev)
            bx = 1 + lt4
            wa = [u_ready[lt4][c] for c in range(NCH)] + [mafl, bankm[bx]]
            for c in range(NCH):
                m_ax = pr.op(TEN, lambda e, bx=bx, c=c, lt4=lt4: e.matmul(
                    pk[bx][:, :], afl_bf[:, c * P:(c + 1) * P],
                    u_bf[:, c * 1024 + lt4 * 512: c * 1024 + (lt4 + 1) * 512],
                    start=(c == 0), stop=(c == NCH - 1)),
                    sig=sT if c == NCH - 1 else None,
                    waits=tuple(x for x in wa if x) if c == 0 else ())
            wwax = [m_ax, m_wev, M.get(("waxw", lt4))]
            m_wax = pr.op(VEC, lambda e, bx=bx, lt4=lt4: e.tensor_tensor(
                wax_bf[:, lt4 * 512:(lt4 + 1) * 512], pk[bx][:, :],
                wexp_sb[:, 0:512], ALU.mult),
                sig=sV, fence=180, waits=tuple(x for x in wwax if x))
            bank_use(bx, m_wax)
            wax_ready[lt4] = m_wax

        r2_ready = [[None] * NCH, [None] * NCH]
        for ochalf in range(2):
            mbw = [None] * NCH
            for c in range(NCH):
                wld = [M.get(("bwTw",))] if c == 0 else []
                mbw[c] = dma(GPS, bwT_bf[:, c * 512:(c + 1) * 512],
                             d_bwT[c * P:(c + 1) * P, ochalf * 512:(ochalf + 1) * 512],
                             sLDg, waits=tuple(x for x in wld if x))
            if h == 0:
                tgt = bwbar_f if ochalf == 0 else bwbar2_f
                for c in range(NCH):
                    m_bwred = pr.op(VEC, lambda e, c=c, tgt=tgt: e.tensor_reduce(
                        tgt[:, c:c + 1], bwT_bf[:, c * 512:(c + 1) * 512], AX.X, ALU.add),
                        sig=sV if c == NCH - 1 else None, waits=(mbw[c],))
                M[("bwred", ochalf)] = m_bwred
            for lt4 in range(2):
                t4 = 2 * h + lt4
                for ocl in range(4):
                    oc = ochalf * 4 + ocl
                    bm = 5 + (lt4 * 4 + ocl) % 2
                    wmain = [u_ready[lt4][c] for c in range(NCH)] + \
                        [wax_ready[lt4], mbfl, bankm[bm]] + [mbw[c] for c in range(NCH)]
                    for c in range(NCH):
                        pr.op(TEN, lambda e, bm=bm, c=c, ocl=ocl, lt4=lt4: e.matmul(
                            pk[bm][:, :], bwT_bf[:, c * 512 + ocl * P: c * 512 + (ocl + 1) * P],
                            u_bf[:, c * 1024 + lt4 * 512: c * 1024 + (lt4 + 1) * 512],
                            start=(c == 0), stop=False),
                            waits=tuple(x for x in wmain if x) if c == 0 else ())
                    m_mm = pr.op(TEN, lambda e, bm=bm, oc=oc, lt4=lt4: e.matmul(
                        pk[bm][:, :], bfl_bf[:, oc * P:(oc + 1) * P],
                        wax_bf[:, lt4 * 512:(lt4 + 1) * 512],
                        start=False, stop=True), sig=sT)
                    wr2 = [m_mm, m_bb, u_ready[lt4][oc], M.get(("r2guard", oc, lt4)),
                           M.get(("dumpB",))]
                    m_r2 = pr.op(VEC, lambda e, bm=bm, oc=oc, lt4=lt4: e.scalar_tensor_tensor(
                        B[:, oc * 1024 + lt4 * 512: oc * 1024 + (lt4 + 1) * 512],
                        pk[bm][:, :], bb_pc[:, oc:oc + 1],
                        A[:, oc * 1024 + lt4 * 512: oc * 1024 + (lt4 + 1) * 512],
                        ALU.add, ALU.add), sig=sV, fence=180,
                        waits=tuple(x for x in wr2 if x))
                    bank_use(bm, m_r2)
                    r2_ready[lt4][oc] = m_r2
            M[("bwTw",)] = m_mm

        if h == 0:
            m_bws = pr.op(VEC, lambda e: e.tensor_tensor(
                bwsum_f[:, :], bwbar_f[:, :], bwbar2_f[:, :], ALU.add),
                sig=sV, waits=(M[("bwred", 0)], M[("bwred", 1)], m_wpbc))
            m_bwc = pr.op(SCA, lambda e: e.copy(bwbar_bf[:, :], bwsum_f[:, :]),
                          sig=sSc, waits=(m_bws,))
            M[("bwbar",)] = m_bwc

        if h == 0 and DUMP:
            M[("dumpR2",)] = M.get(("dbgwun",))
        for lt4 in range(2):
            t4 = 2 * h + lt4
            wm = [M[("bwbar",)], m_bflbc, wax_ready[lt4], bankm[7]] + \
                [u_ready[lt4][c] for c in range(NCH)]
            for c in range(NCH):
                pr.op(TEN, lambda e, c=c, lt4=lt4: e.matmul(
                    pk[7][0:1, :], bwbar_bf[:, c:c + 1],
                    u_bf[:, c * 1024 + lt4 * 512: c * 1024 + (lt4 + 1) * 512],
                    start=(c == 0), stop=False),
                    waits=tuple(x for x in wm if x) if c == 0 else ())
            m_lastmean = pr.op(TEN, lambda e, lt4=lt4: e.matmul(
                pk[7][0:1, :], bflbar_bf[:, 0:1], wax_bf[:, lt4 * 512:(lt4 + 1) * 512],
                start=False, stop=True), sig=sT)
            M[("ubfw", 0, lt4)] = m_lastmean
            M[("ubfw", 1, lt4)] = m_lastmean
            for c in range(NCH):
                M[("ubfw", c, lt4)] = m_lastmean
            M[("waxw", lt4)] = m_lastmean
            m_m = pr.op(VEC, lambda e: e.tensor_scalar(
                mrow2[:, :], pk[7][0:1, :], bsum_b[0:1, 0:1], 1.0 / 1024.0, ALU.add, ALU.mult),
                sig=sV, fence=180, waits=(m_lastmean, m_bsv2))
            bank_use(7, m_m)
            m_sq = [None] * NCH
            for c in range(NCH):
                m_sq[c] = pr.op(SCA, lambda e, c=c, lt4=lt4: e.activation(
                    sqsc[:, c * 512:(c + 1) * 512],
                    B[:, c * 1024 + lt4 * 512: c * 1024 + (lt4 + 1) * 512], AF.Square),
                    sig=sSc,
                    waits=tuple(x for x in (r2_ready[lt4][c], scrm.get(("sq", c))) if x))
                M[("r1guard", c, lt4)] = m_sq[c]
            for c in range(NCH):
                mmv = pr.op(TEN, lambda e, c=c: e.matmul(
                    pk[7][0:1, :], ones_bf[:, 0:1], sqsc[:, c * 512:(c + 1) * 512],
                    start=(c == 0), stop=(c == NCH - 1)),
                    sig=sT,
                    waits=tuple(x for x in (m_sq[c], bankm[7] if c == 0 else None) if x))
                scrm[("sq", c)] = mmv
            m_sv = mmv
            m_msq = pr.op(SCA, lambda e: e.activation(
                msrow[:, :], mrow2[:, :], AF.Square), sig=sSc,
                waits=tuple(x for x in (m_m, M.get(("msrowr",))) if x))
            m_v = pr.op(VEC, lambda e: e.scalar_tensor_tensor(
                vrow[:, :], pk[7][0:1, :], 1.0 / 1024.0, msrow[:, :], ALU.mult, ALU.subtract),
                sig=sV, fence=180,
                waits=tuple(x for x in (m_sv, m_msq, M.get(("vrowr",))) if x))
            bank_use(7, m_v)
            m_l = pr.op(SCA, lambda e: e.activation(srow[:, :], vrow[:, :], AF.Ln,
                                                    bias=epsln[0:1, 0:1]),
                        sig=sSc, waits=(m_v,))
            m_nh = pr.op(VEC, lambda e: e.tensor_scalar_mul(srow2[:, :], srow[:, :], -0.5),
                         sig=sV, waits=(m_l,))
            m_s = pr.op(SCA, lambda e: e.activation(srow[:, :], srow2[:, :], AF.Exp),
                        sig=sSc, waits=(m_nh,))
            m_ms = pr.op(VEC, lambda e: e.tensor_tensor(msrow[:, :], mrow2[:, :], srow[:, :],
                                                        ALU.mult), sig=sV, waits=(m_s,))
            m_bm = pr.op(TEN, lambda e: e.matmul(
                pk[1][:, :], onesr_f[0:1, 0:P], msrow[0:1, :], start=True, stop=True),
                sig=sT, waits=tuple(x for x in (m_ms, bankm[1]) if x))
            m_bs = pr.op(TEN, lambda e: e.matmul(
                pk[2][:, :], onesr_f[0:1, 0:P], srow[0:1, :], start=True, stop=True),
                sig=sT, waits=tuple(x for x in (bankm[2],) if x))
            lastp2 = None
            for c in range(NCH):
                scr = c % 2
                wap = [m_bs, m_bm, r2_ready[lt4][c], scrm.get(("ts", scr))]
                m_p1 = pr.op(VEC, lambda e, c=c, lt4=lt4, scr=scr: e.tensor_tensor(
                    tsc[:, 0:512],
                    B[:, c * 1024 + lt4 * 512: c * 1024 + (lt4 + 1) * 512],
                    pk[2][:, :], ALU.mult), sig=sV, waits=tuple(x for x in wap if x))
                wy = [scrm.get(("ys", scr))]
                m_p2 = pr.op(VEC, lambda e, scr=scr: e.tensor_tensor(
                    ysc[:, scr * 512:(scr + 1) * 512], tsc[:, 0:512],
                    pk[1][:, :], ALU.subtract), sig=sV, waits=tuple(x for x in wy if x))
                scrm[("ts", scr)] = m_p2
                m_st = dma(SYN, d_y[c * P:(c + 1) * P, t4 * 512:(t4 + 1) * 512],
                           ysc[:, scr * 512:(scr + 1) * 512], sOUT, waits=(m_p2,))
                scrm[("ys", scr)] = m_st
                y_total += 16
                lastp2 = m_p2
            bank_use(1, lastp2)
            bank_use(2, lastp2)
        for c in range(NCH):
            M[("Afree", c)] = r2_ready[1][c]

    pr.op(SYN, lambda e, v=None: e.wait_ge(sOUT.h, sOUT.n))

    _mk_blocks(nc, pr)
    es4.close()
    es.close()
    return nc


def _mk_blocks(nc, pr):
    with nc.Block() as block:
        @block.tensor
        def _(eng):
            for f in pr.q["tensor"]:
                f(eng)

        @block.vector
        def _(eng):
            for f in pr.q["vector"]:
                f(eng)

        @block.scalar
        def _(eng):
            for f in pr.q["scalar"]:
                f(eng)

        @block.gpsimd
        def _(eng):
            for f in pr.q["gpsimd"]:
                f(eng)

        @block.sync
        def _(eng):
            for f in pr.q["sync"]:
                f(eng)


_NC = None


def _get_nc():
    global _NC
    if _NC is None:
        _NC = build_nc()
    return _NC


def kernel(L0_attn_x, L0_attn_w_qkv, L0_attn_b_qkv, L0_attn_w_proj, L0_attn_b_proj,
           L0_ffn_base_w, L0_ffn_base_b, L0_ffn_lora_A, L0_ffn_lora_B,
           L0_ffn_router_w, L0_ffn_router_b):
    x = np.ascontiguousarray(np.asarray(L0_attn_x, dtype=np.float32).reshape(16384, 1024))
    wqkv = np.ascontiguousarray(np.asarray(L0_attn_w_qkv, dtype=np.float32))
    bqkv = np.asarray(L0_attn_b_qkv, dtype=np.float32)
    wproj = np.ascontiguousarray(np.asarray(L0_attn_w_proj, dtype=np.float32))
    bproj = np.asarray(L0_attn_b_proj, dtype=np.float32)
    bw = np.asarray(L0_ffn_base_w, dtype=np.float32)
    bb = np.asarray(L0_ffn_base_b, dtype=np.float32)
    lA = np.asarray(L0_ffn_lora_A, dtype=np.float32)
    lB = np.asarray(L0_ffn_lora_B, dtype=np.float32)
    rw = np.asarray(L0_ffn_router_w, dtype=np.float32)
    rb = np.asarray(L0_ffn_router_b, dtype=np.float32)

    common = {
        "wqkv": wqkv,
        "bq_pc": np.ascontiguousarray(bqkv[0:1024].reshape(8, 128).T),
        "bkv_row": np.ascontiguousarray(bqkv[1024:3072].reshape(1, 2048)),
        "wproj": wproj,
        "bproj_pc": np.ascontiguousarray(bproj.reshape(8, 128).T),
        "bwT": np.ascontiguousarray(bw.T),
        "bb_pc": np.ascontiguousarray(bb.reshape(8, 128).T),
        "afl": np.ascontiguousarray(lA.transpose(1, 0, 2).reshape(1024, 128)),
        "bfl": np.ascontiguousarray(lB.reshape(128, 1024)),
        "rwT": np.ascontiguousarray(rw.T),
        "rb_pc": np.ascontiguousarray(rb.reshape(8, 1)),
    }
    in_maps = []
    for c in range(8):
        m = dict(common)
        m["xT"] = np.ascontiguousarray(x[c * TOK:(c + 1) * TOK, :].T)
        in_maps.append(m)

    nc = _get_nc()
    res = run_bass_kernel_spmd(nc, in_maps, core_ids=list(range(8)))
    parts = [np.ascontiguousarray(np.asarray(r["y"]).T) for r in res.results]
    return np.concatenate(parts, axis=0).reshape(2, 8192, 1024).astype(np.float32)
